# version 17
# baseline (speedup 1.0000x reference)
import numpy as np
import ml_dtypes
import concourse.bass as bass
import concourse.mybir as mybir
from concourse.bass_utils import run_bass_kernel_spmd

F32 = mybir.dt.float32
BF16 = mybir.dt.bfloat16
AF = mybir.ActivationFunctionType
ALU = mybir.AluOpType
AX = mybir.AxisListType.X

QK_SCALE = float(2.0 ** -3.5)
LN_EPS = 1e-5
LORA_SCALING = 2.0


class Plan:
    ENGINES = ("tensor", "vector", "scalar", "gpsimd", "sync")

    def __init__(self, nc):
        self.nc = nc
        self.q = {e: [] for e in self.ENGINES}
        self.count = {}
        self.auto = {}
        self.last = {e: None for e in self.ENGINES}

    def sem(self, name):
        h = self.nc.alloc_semaphore(name)
        self.count[h] = 0
        return h

    def set_auto(self, eng, sem):
        self.auto[eng] = sem

    def do(self, eng, method, *args, wait=None, inc=None, chain=True, **kwargs):
        waits = [w for w in (wait or ()) if w is not None]
        if eng in self.auto:
            if chain and self.last[eng] is not None:
                waits.append(self.last[eng])
            if inc is None:
                inc = (self.auto[eng], 1)
        best = {}
        for (s, v) in waits:
            if best.get(s, -1) < v:
                best[s] = v
        waits = tuple(best.items())
        tgt = None
        if inc is not None:
            s, n = inc
            self.count[s] += n
            tgt = (s, self.count[s])
            if eng in self.auto and s == self.auto[eng]:
                self.last[eng] = tgt

        def fn(E, method=method, args=args, kwargs=kwargs, waits=waits, inc=inc):
            for (s, v) in waits:
                E.wait_ge(s, v)
            ins = getattr(E, method)(*args, **kwargs)
            if inc is not None:
                ins.then_inc(inc[0], inc[1])
            return ins

        self.q[eng].append(fn)
        return tgt

    def wait(self, eng, *events):
        evs = tuple(e for e in events if e is not None)

        def fn(E, evs=evs):
            for (s, v) in evs:
                E.wait_ge(s, v)

        self.q[eng].append(fn)

    def emit(self, name=None):
        nc = self.nc
        with nc.Block(name) as block:
            for eng in self.ENGINES:
                queue = self.q[eng]
                if not queue:
                    continue
                getattr(block, eng)(lambda E, queue=queue: [fn(E) for fn in queue])
        self.q = {e: [] for e in self.ENGINES}


def ap(t, offset, dims):
    return bass.AP(t, offset, [list(d) for d in dims])


def build(NQ, T, dbg=False):
    NKV = NQ // 4
    DM = NQ * 128
    DKV = NKV * 128
    S = 4 * T
    CH = S // 128
    TC = T // 128
    DCH = DM // 128
    QC = [(o, min(512, T - o)) for o in range(0, T, 512)]
    NQC = len(QC)
    OW = min(512, DM)
    OG = DM // OW
    NB = DM // 512

    nc = bass.Bass("TRN2", target_bir_lowering=False)

    x_d = nc.dram_tensor("x", [T, DM], F32, kind="ExternalInput")
    wqT = nc.dram_tensor("wqT", [DM, DM], F32, kind="ExternalInput")
    wkT = nc.dram_tensor("wkT", [DM, DKV], F32, kind="ExternalInput")
    wvT = nc.dram_tensor("wvT", [DM, DKV], F32, kind="ExternalInput")
    bv_d = nc.dram_tensor("bv", [1, DKV], F32, kind="ExternalInput")
    woT = nc.dram_tensor("woT", [DM, DM], F32, kind="ExternalInput")
    wbT = nc.dram_tensor("wbT", [DM, DM], F32, kind="ExternalInput")
    bb_d = nc.dram_tensor("bb", [1, DM], F32, kind="ExternalInput")
    wrT = nc.dram_tensor("wrT", [DM, 8], F32, kind="ExternalInput")
    at_d = nc.dram_tensor("at", [DM, 128], F32, kind="ExternalInput")
    bstk_d = nc.dram_tensor("bstk", [128, DM], F32, kind="ExternalInput")
    out_d = nc.dram_tensor("out", [T, DM], F32, kind="ExternalOutput")

    kin = nc.dram_tensor("kin", [DKV, T], BF16)
    vin = nc.dram_tensor("vin", [T, DKV], BF16)
    kall = nc.dram_tensor("kall", [4 * DKV, T], BF16)
    vall = nc.dram_tensor("vall", [4 * T, DKV], BF16)
    xp = nc.dram_tensor("xp", [T, DM], F32)

    ident_bf = nc.inline_tensor(np.eye(128, dtype=ml_dtypes.bfloat16), name="ident_bf")
    ident_f = nc.inline_tensor(np.eye(128, dtype=np.float32), name="ident_f")
    allones = nc.inline_tensor(np.ones((128, 128), dtype=ml_dtypes.bfloat16), name="allones")
    allonesf = nc.inline_tensor(np.ones((128, 128), dtype=np.float32), name="allonesf")
    sel_np = np.zeros((8, 128), dtype=np.float32)
    for e in range(8):
        sel_np[e, e * 16:(e + 1) * 16] = 1.0
    sel_c = nc.inline_tensor(sel_np, name="sel")
    onescol = nc.inline_tensor(np.ones((1, 128), dtype=np.float32), name="onescol")

    A = nc.alloc_sbuf_tensor
    idb = A("idb", [128, 128], BF16)
    idf = A("idf", [128, 128], F32)
    onesb = A("onesb", [128, 128], BF16)
    onesf = A("onesf", [128, 128], F32)
    selb = A("selb", [8, 128], F32)
    ones1 = A("ones1", [1, 128], F32)
    bv_col = A("bv_col", [128, NKV], F32)
    wr_sb = A("wr_sb", [128, DCH * 8], BF16)
    hw_all = A("hw_all", [128, TC * 128], BF16)
    rowstg = A("rowstg", [1, 512], F32)
    bbog = A("bbog", [128, OW], F32)
    recip = A("recip", [128, T], F32)
    accP = A("accP", [128, T], F32)
    expb = A("expb", [128, 4 * T], BF16)
    _expb_addr = nc.lookup_mloc(expb).addr
    vfm_sb = nc.alloc_sbuf_tensor_at("vfm_sb", [128, T], BF16, offset=_expb_addr)
    ktile_sb = nc.alloc_sbuf_tensor_at("ktile_sb", [128, T], BF16, offset=_expb_addr + T * 2)
    w16s = A("w16s", [128, 128], F32)
    w8T_sb = A("w8T_sb", [8, 128], F32)
    bn6 = A("bn6", [128, 6 * NB], F32)
    bn6b = A("bn6b", [128, TC * OG * 6], F32)
    bn2 = A("bn2", [128, 2], F32)
    m2v = A("m2v", [128, TC], F32)
    r2v = A("r2v", [128, TC], F32)
    s_a = A("s_a", [128, 8], F32)
    s_b = A("s_b", [128, 8], F32)
    s_c = A("s_c", [128, 8], F32)
    s_m1 = A("s_m1", [128, 1], F32)
    s_m2 = A("s_m2", [128, 1], F32)
    s_dd = A("s_dd", [128, 1], F32)
    s_w1 = A("s_w1", [128, 1], F32)
    s_w2 = A("s_w2", [128, 1], F32)
    s_v1 = A("s_v1", [128, 1], F32)
    s_r1 = A("s_r1", [128, 1], F32)
    eps_sb = A("eps_sb", [128, 1], F32)

    BIG = max(NQ * T, 2 * DCH * OW)
    bufA = A("bufA", [128, BIG], BF16)
    bufB = A("bufB", [128, BIG], BF16)
    R3B = max(DM * 4, S * 2 * 2)
    xtile = A("xtile", [128, R3B // 4], F32)
    r3_addr = nc.lookup_mloc(xtile).addr
    kbuf = nc.alloc_sbuf_tensor_at("kbuf", [128, S], BF16, offset=r3_addr)
    vbuf = nc.alloc_sbuf_tensor_at("vbuf", [128, S], BF16, offset=r3_addr + S * 2)
    R4B = max(2 * DM * 2, 8 * OW * 4)
    wqbuf = A("wqbuf", [128, R4B // 2], BF16)
    r4_addr = nc.lookup_mloc(wqbuf).addr
    kbuf2 = nc.alloc_sbuf_tensor_at("kbuf2", [128, S], BF16, offset=r4_addr)
    vbuf2 = nc.alloc_sbuf_tensor_at("vbuf2", [128, S], BF16, offset=r4_addr + S * 2)
    xsl = nc.alloc_sbuf_tensor_at("xsl", [128, 4 * OW], F32, offset=r4_addr)
    osl = nc.alloc_sbuf_tensor_at("osl", [128, 4 * OW], F32, offset=r4_addr + 4 * OW * 4)
    at_sb = nc.alloc_sbuf_tensor_at("at_sb", [128, DCH * 128], BF16, offset=r4_addr)
    R5B = max(2 * DM * 2, TC * DKV * 2, DM * 2)
    lnt = A("lnt", [128, R5B // 2], BF16)
    r5_addr = nc.lookup_mloc(lnt).addr
    vtm = nc.alloc_sbuf_tensor_at("vtm", [128, TC * DKV], BF16, offset=r5_addr)
    bstk_sb = nc.alloc_sbuf_tensor_at("bstk_sb", [128, DM], BF16, offset=r5_addr)

    ps = nc.alloc_psum_tensor("ps", [128, 3072], F32)
    pst = nc.alloc_psum_tensor("pst", [128, 2048], BF16)

    P = Plan(nc)
    s_w = P.sem("s_w")
    s_wA = P.sem("s_wA")
    s_wB = P.sem("s_wB")
    s_xt = P.sem("s_xt")
    s_kb = [P.sem("s_kb0"), P.sem("s_kb1")]
    s_vb = [P.sem("s_vb0"), P.sem("s_vb1")]
    s_sl = [P.sem(f"s_sl{i}") for i in range(4)]
    s_os = [P.sem(f"s_os{i}") for i in range(4)]
    s_pe = P.sem("s_pe")
    s_act = P.sem("s_act")
    s_dve = P.sem("s_dve")
    s_cc = P.sem("s_cc")
    s_kvw = P.sem("s_kvw")
    P.set_auto("vector", s_dve)
    P.set_auto("scalar", s_act)

    for dst, src in ((idb, ident_bf), (idf, ident_f), (onesb, allones),
                     (onesf, allonesf), (selb, sel_c), (ones1, onescol)):
        P.do("gpsimd", "dma_start", dst[:, :], src.ap(), inc=(s_w, 16))
    P.do("gpsimd", "dma_start",
         bv_col[:, :], ap(bv_d, 0, [[1, 128], [128, NKV], [1, 1]]),
         allow_slow_non_contiguous=True, inc=(s_w, 16))
    e_consts = P.do("gpsimd", "dma_start",
                    wr_sb[:, :], ap(wrT, 0, [[8, 128], [128 * 8, DCH], [1, 8]]),
                    inc=(s_w, 16))
    P.do("vector", "memset", eps_sb[:, :], LN_EPS)
    P.wait("gpsimd", e_consts)
    blocks = [("q", i, wqT, DM) for i in range(NQ)] + \
             [("k", i, wkT, DKV) for i in range(NKV)] + \
             [("v", i, wvT, DKV) for i in range(NKV)]
    e_blk = [None] * len(blocks)
    last_use = [None] * len(blocks)

    def emit_blk_dma(b):
        kind, i, wt, wcols = blocks[b]
        slot = b % 2
        e_blk[b] = P.do("gpsimd", "dma_start",
                        ap(wqbuf, slot * DM, [[R4B // 2, 128], [128, DCH], [1, 128]]),
                        ap(wt, i * 128, [[wcols, 128], [128 * wcols, DCH], [1, 128]]),
                        wait=[last_use[b - 2] if b >= 2 else None],
                        inc=(s_wA if slot == 0 else s_wB, 16))

    emit_blk_dma(0)
    if len(blocks) > 1:
        emit_blk_dma(1)
    P.emit("consts")

    tile_done = [None] * TC
    half_evac = [None, None]
    prev_x = None
    for tc in range(TC):
        lslot = tc % 2
        e_xt = P.do("sync", "dma_start", xtile[:, 0:DM],
                    ap(x_d, tc * 128 * DM, [[DM, 128], [1, DM]]),
                    wait=[prev_x], inc=(s_xt, 16))
        eb = None
        for k in range(NB):
            eb = P.do("vector", "bn_stats", bn6[:, k * 6:(k + 1) * 6],
                      xtile[:, k * 512:(k + 1) * 512],
                      wait=[e_xt] if k == 0 else None)
        e_ag = P.do("vector", "bn_aggr", bn2[:, :], bn6[:, :])
        e_sq = P.do("scalar", "activation", s_v1[:, :], bn2[:, 1:2], AF.Sqrt,
                    bias=eps_sb[:, :], wait=[e_ag])
        e_rc = P.do("vector", "reciprocal", s_r1[:, :], s_v1[:, :], wait=[e_sq])
        e_aff = P.do("vector", "tensor_scalar", lnt[:, lslot * DM:lslot * DM + DM],
                     xtile[:, 0:DM], bn2[:, 0:1], s_r1[:, :], ALU.subtract, ALU.mult,
                     wait=[tile_done[tc - 2] if tc >= 2 else None])
        prev_x = e_aff
        last_ev = None
        for g0 in range(0, DCH, 8):
            gn = min(8, DCH - g0)
            grp = (tc * ((DCH + 7) // 8) + g0 // 8) % 2
            e_tp = None
            for j in range(gn):
                c = g0 + j
                e_tp = P.do("tensor", "transpose",
                            pst[:, grp * 1024 + j * 128: grp * 1024 + (j + 1) * 128],
                            lnt[:, lslot * DM + c * 128: lslot * DM + (c + 1) * 128],
                            idb[:, :],
                            wait=[e_aff, half_evac[grp]] if j == 0 else None,
                            inc=(s_pe, 1) if j == gn - 1 else None)
            e_ev = P.do("vector", "tensor_copy",
                        ap(bufA, g0 * T + tc * 128, [[BIG, 128], [T, gn], [1, 128]]),
                        ap(pst, grp * 1024, [[2048, 128], [128, gn], [1, 128]]),
                        wait=[e_tp])
            half_evac[grp] = e_ev
            last_ev = e_ev
        tile_done[tc] = last_ev
    e_ln_done = tile_done[TC - 1]

    ps_evac = [None, None, None, None]
    k_dma_prev = None
    v_tp_done = None
    vhalf_evac = [e for e in half_evac]
    vt_dma = [None] * TC
    for b, (kind, i, wt, wcols) in enumerate(blocks):
        tile_evs = []
        base = (b % 2) * 2
        e_mm = None
        for c in range(DCH):
            for qi, (qo, qw) in enumerate(QC):
                slot = base + qi
                w0 = None
                if c == 0:
                    w0 = [e_blk[b], e_ln_done, ps_evac[slot]] if qi == 0 else [ps_evac[slot]]
                e_mm = P.do("tensor", "matmul", ps[:, slot * 512: slot * 512 + qw],
                            ap(wqbuf, (b % 2) * DM + c * 128, [[R4B // 2, 128], [1, 128]]),
                            ap(bufA, c * T + qo, [[BIG, 128], [1, qw]]),
                            start=(c == 0), stop=(c == DCH - 1),
                            wait=w0,
                            inc=(s_pe, 1) if (c == DCH - 1 and qi == NQC - 1) else None)
        last_use[b] = e_mm
        for qi, (qo, qw) in enumerate(QC):
            slot = base + qi
            if kind == "q":
                e_ev = P.do("scalar", "copy",
                            ap(bufB, i * T + qo, [[BIG, 128], [1, qw]]),
                            ps[:, slot * 512: slot * 512 + qw],
                            wait=[e_mm])
            elif kind == "k":
                e_ev = P.do("scalar", "copy",
                            ap(ktile_sb, qo, [[T, 128], [1, qw]]),
                            ps[:, slot * 512: slot * 512 + qw],
                            wait=[e_mm, k_dma_prev if qo == 0 else None])
            else:
                e_ev = P.do("vector", "tensor_scalar_add",
                            ap(vfm_sb, qo, [[T, 128], [1, qw]]),
                            ps[:, slot * 512: slot * 512 + qw],
                            bv_col[:, i:i + 1],
                            wait=[e_mm, v_tp_done if qo == 0 else None])
            ps_evac[slot] = e_ev
            tile_evs.append(e_ev)
        if b + 2 < len(blocks):
            emit_blk_dma(b + 2)
        if kind == "k":
            k_dma_prev = P.do("sync", "dma_start",
                              ap(kin, i * 128 * T, [[T, 128], [1, T]]),
                              ktile_sb[:, :], wait=[tile_evs[-1]], inc=(s_kvw, 16))
        elif kind == "v":
            for tcsub in range(TC):
                grp = tcsub % 2
                w0 = [tile_evs[-1], vhalf_evac[grp]]
                if i == 0 and tcsub < 2:
                    w0.append(e_ln_done)
                e_tp = P.do("tensor", "transpose",
                            pst[:, grp * 1024: grp * 1024 + 128],
                            vfm_sb[:, tcsub * 128:(tcsub + 1) * 128], idb[:, :],
                            wait=w0, inc=(s_pe, 1))
                v_tp_done = e_tp
                e_ev2 = P.do("vector", "tensor_copy",
                             ap(vtm, tcsub * DKV + i * 128, [[TC * DKV, 128], [1, 128]]),
                             pst[:, grp * 1024: grp * 1024 + 128],
                             wait=[e_tp, vt_dma[tcsub] if i == NKV - 1 else None])
                vhalf_evac[grp] = e_ev2
                if i == NKV - 1:
                    vt_dma[tcsub] = P.do("sync", "dma_start",
                                         ap(vin, tcsub * 128 * DKV, [[DKV, 128], [1, DKV]]),
                                         vtm[:, tcsub * DKV:(tcsub + 1) * DKV],
                                         wait=[e_ev2], inc=(s_kvw, 16))

    P.wait("gpsimd", (s_kvw, P.count[s_kvw]))
    e_cck = P.do("gpsimd", "collective_compute", "AllGather", ALU.bypass,
                 [[0, 1, 2, 3], [4, 5, 6, 7]], [kin.ap()], [kall.ap()], inc=(s_cc, 1))
    e_ccv = P.do("gpsimd", "collective_compute", "AllGather", ALU.bypass,
                 [[0, 1, 2, 3], [4, 5, 6, 7]], [vin.ap()], [vall.ap()], inc=(s_cc, 1))
    P.emit("p1")

    s_pool = P.sem("s_pool")
    P.set_auto("gpsimd", s_pool)
    exp_ev = {}
    acc_ev = {}
    gc = 0
    e_kb = e_vb = None
    prev_ctx_evac = None
    prev_recip = None
    prev_e_sm = None
    pend_pe = []

    def flush_pe():
        while pend_pe:
            pend_pe.pop(0)()

    last_score = None
    kvbufs = [(kbuf, vbuf), (kbuf2, vbuf2)]
    e_kv_slot = [None, None]

    def emit_kv_dma(kv, wait_k, wait_v):
        kb, vb = kvbufs[kv % 2]
        ek = P.do("sync", "dma_start", kb[:, :],
                  ap(kall, kv * 128 * T, [[T, 128], [DKV * T, 4], [1, T]]),
                  wait=[wait_k, e_cck, e_ccv], inc=(s_kb[kv % 2], 16))
        ev = P.do("sync", "dma_start", vb[:, :],
                  ap(vall, kv * 128, [[DKV, 128], [128 * DKV, CH], [1, 128]]),
                  wait=[wait_v], inc=(s_vb[kv % 2], 16))
        e_kv_slot[kv % 2] = (ek, ev)

    emit_kv_dma(0, None, None)
    for h in range(NQ):
        kvh = h // 4
        kbuf_c, vbuf_c = kvbufs[kvh % 2]
        if h % 4 == 0:
            flush_pe()
            if kvh + 1 < NKV:
                emit_kv_dma(kvh + 1, last_score, (s_pe, P.count[s_pe]))
            e_kb, e_vb = e_kv_slot[kvh % 2]
        accV_first = True
        accP_first = True
        for c in range(CH):
            i = gc
            sl = (i % 2) * T
            wsc = []
            if c == 0:
                wsc += [e_kb, e_vb]
            if c == 1:
                wsc.append(prev_recip)
            if c == 2 and gc >= CH:
                pass
            e_sc = None
            for (qo, qw) in QC:
                e_sc = P.do("tensor", "matmul",
                            ps[:, sl + qo: sl + qo + qw],
                            kbuf_c[:, c * 128:(c + 1) * 128],
                            ap(bufB, h * T + qo, [[BIG, 128], [1, qw]]),
                            start=True, stop=True,
                            wait=(wsc + [exp_ev.get(i - 2)]) if qo == 0 else None,
                            inc=(s_pe, 1) if qo == QC[-1][0] else None)
            last_score = e_sc
            exp_ev[i] = P.do("scalar", "activation",
                             expb[:, (i % 4) * T:(i % 4) * T + T],
                             ps[:, sl: sl + T],
                             AF.Exp, scale=QK_SCALE,
                             wait=[e_sc, acc_ev.get(i - 4)], chain=False)
            if c <= 1:
                flush_pe()
            else:
                while len(pend_pe) > 2:
                    pend_pe.pop(0)()

            def mk_pv(c=c, i=i, h=h, vbuf_c=vbuf_c):
                def emit():
                    need_exp = (c <= 1 or c >= CH - 2)
                    for (qo, qw) in QC:
                        P.do("tensor", "matmul",
                             ps[:, 2048 + qo: 2048 + qo + qw],
                             vbuf_c[:, c * 128:(c + 1) * 128],
                             expb[:, (i % 4) * T + qo:(i % 4) * T + qo + qw],
                             start=(c == 0), stop=(c == CH - 1),
                             wait=([exp_ev[i] if need_exp else None,
                                    prev_ctx_evac if c == 0 else None]
                                   if qo == 0 else None),
                             inc=(s_pe, 1) if (c == CH - 1 and qo == QC[-1][0]) else None)
                return emit
            pend_pe.append(mk_pv())
            if c % 4 == 1:
                if accP_first:
                    P.do("gpsimd", "memset", accP[:, 0:T], 0.0,
                         wait=[prev_e_sm])
                    acc_ev[i] = P.do("gpsimd", "tensor_tensor", accP[:, 0:T],
                                     accP[:, 0:T],
                                     expb[:, (i % 4) * T:(i % 4) * T + T], ALU.add,
                                     wait=[exp_ev[i]])
                    accP_first = False
                else:
                    acc_ev[i] = P.do("gpsimd", "tensor_tensor", accP[:, 0:T],
                                     accP[:, 0:T],
                                     expb[:, (i % 4) * T:(i % 4) * T + T], ALU.add,
                                     wait=[exp_ev[i]])
            else:
                if accV_first:
                    P.do("vector", "memset", recip[:, 0:T], 0.0)
                    acc_ev[i] = P.do("vector", "tensor_tensor", recip[:, 0:T],
                                     recip[:, 0:T],
                                     expb[:, (i % 4) * T:(i % 4) * T + T], ALU.add,
                                     wait=[exp_ev[i]])
                    accV_first = False
                else:
                    acc_ev[i] = P.do("vector", "tensor_tensor", recip[:, 0:T],
                                     recip[:, 0:T],
                                     expb[:, (i % 4) * T:(i % 4) * T + T], ALU.add,
                                     wait=[exp_ev[i]])
            gc += 1
        eV = acc_ev[gc - 1]
        eP = acc_ev[gc - 3]
        e31, e30 = exp_ev[gc - 1], exp_ev[gc - 2]

        def mk_sums(eV=eV, eP=eP, e31=e31, e30=e30, holder=None):
            def emit():
                e_sm = None
                for (qo, qw) in QC:
                    P.do("tensor", "matmul", ps[:, 1024 + qo: 1024 + qo + qw],
                         onesf[:, :], recip[:, qo: qo + qw], start=True, stop=False,
                         wait=[eV, eP, e31, e30] if qo == 0 else None)
                    e_sm = P.do("tensor", "matmul", ps[:, 1024 + qo: 1024 + qo + qw],
                                onesf[:, :], accP[:, qo: qo + qw], start=False,
                                stop=True, inc=(s_pe, 1))
                holder["e"] = e_sm
            return emit

        sums_holder = {}
        pend_pe.append(mk_sums(holder=sums_holder))

        def v_tail(h=h, sums_holder=sums_holder):
            e_sm = sums_holder["e"]
            e_ln = P.do("scalar", "activation", accP[:, 0:T],
                        ps[:, 1024:1024 + T], AF.Ln, wait=[e_sm], chain=False)
            e_rcp = P.do("scalar", "activation", recip[:, 0:T],
                         accP[:, 0:T], AF.Exp, scale=-1.0, wait=[e_ln], chain=False)
            e_evac = P.do("vector", "tensor_tensor",
                          ap(bufA, h * T, [[BIG, 128], [1, T]]),
                          ps[:, 2048:2048 + T], recip[:, 0:T], ALU.mult,
                          wait=[e_rcp, (s_pe, P.count[s_pe])])
            return e_ln, e_rcp, e_evac

        def mk_vtail(fn=v_tail, sums_holder=sums_holder):
            def emit():
                nonlocal prev_recip, prev_ctx_evac, prev_e_sm
                e_ln, e_rcp, e_evac = fn()
                prev_recip = e_ln
                prev_e_sm = e_rcp
                prev_ctx_evac = e_evac
            return emit
        pend_pe.append(mk_vtail())
    flush_pe()
    e_wo = [None, None]
    lastuse_wo = [None, None]

    def emit_wo_dma(og):
        slot = og % 2
        e_wo[slot] = P.do("gpsimd", "dma_start",
                          ap(bufB, slot * (DCH * OW), [[BIG, 128], [OW, DCH], [1, OW]]),
                          ap(woT, og * OW, [[DM, 128], [128 * DM, DCH], [1, OW]]),
                          wait=[lastuse_wo[slot], last_score], inc=(s_wA if slot == 0 else s_wB, 16))

    emit_wo_dma(0)
    if OG > 1:
        emit_wo_dma(1)
    P.emit("p2")

    xs_dma = [None] * 4
    xs_user = [None] * 4
    os_war = [None] * 4
    ps_war = [None] * 4
    bn_last = None
    for og in range(OG):
        for tc in range(TC):
            j = og * TC + tc
            sl = j % 4
            xs_dma[sl] = P.do("sync", "dma_start",
                              xsl[:, sl * OW:(sl + 1) * OW],
                              ap(x_d, tc * 128 * DM + og * OW, [[DM, 128], [1, OW]]),
                              wait=[xs_user[sl]], inc=(s_sl[sl], 16))
            e_mm = None
            for c in range(DCH):
                e_mm = P.do("tensor", "matmul",
                            ps[:, (j % 4) * 512:(j % 4) * 512 + OW],
                            ap(bufA, c * T + tc * 128, [[BIG, 128], [1, 128]]),
                            ap(bufB, (og % 2) * (DCH * OW) + c * OW, [[BIG, 128], [1, OW]]),
                            start=(c == 0), stop=(c == DCH - 1),
                            wait=[e_wo[og % 2], ps_war[j % 4]] if c == 0 else None,
                            inc=(s_pe, 1) if c == DCH - 1 else None)
            lastuse_wo[og % 2] = e_mm
            if og + 2 < OG and tc == TC - 1:
                emit_wo_dma(og + 2)
            e_xp = P.do("vector", "tensor_tensor",
                        osl[:, sl * OW:(sl + 1) * OW],
                        ps[:, (j % 4) * 512:(j % 4) * 512 + OW],
                        xsl[:, sl * OW:(sl + 1) * OW], ALU.add,
                        wait=[e_mm, xs_dma[sl], os_war[sl]])
            ps_war[j % 4] = e_xp
            xs_user[sl] = e_xp
            jj = tc * OG + og
            e_bn = P.do("vector", "bn_stats",
                        bn6b[:, jj * 6:(jj + 1) * 6],
                        osl[:, sl * OW:(sl + 1) * OW])
            bn_last = e_bn
            os_war[sl] = P.do("sync", "dma_start",
                              ap(xp, tc * 128 * DM + og * OW, [[DM, 128], [1, OW]]),
                              osl[:, sl * OW:(sl + 1) * OW],
                              wait=[e_bn], inc=(s_os[sl], 16))
    prev_sq = None
    for tc in range(TC):
        e_ag = P.do("vector", "bn_aggr", bn2[:, :],
                    bn6b[:, tc * OG * 6:(tc + 1) * OG * 6],
                    wait=[bn_last, prev_sq])
        e_mv = P.do("vector", "tensor_copy", m2v[:, tc:tc + 1], bn2[:, 0:1])
        e_sq = P.do("scalar", "activation", s_v1[:, :], bn2[:, 1:2], AF.Sqrt,
                    bias=eps_sb[:, :], wait=[e_mv])
        prev_sq = e_sq
        P.do("vector", "reciprocal", r2v[:, tc:tc + 1], s_v1[:, :], wait=[e_sq])
    xp_written = tuple((s, P.count[s]) for s in s_os)
    P.emit("p3")

    e_at = P.do("gpsimd", "dma_start",
                at_sb[:, :], ap(at_d, 0, [[128, 128], [128 * 128, DCH], [1, 128]]),
                inc=(s_w, 16))
    prev_x4 = None
    half4 = [None, None]
    tile_done4 = [None] * TC
    rtr_pe_last = None
    e_hw = [None] * TC
    for tc in range(TC):
        lslot = tc % 2
        e_xt = P.do("sync", "dma_start", xtile[:, 0:DM],
                    ap(xp, tc * 128 * DM, [[DM, 128], [1, DM]]),
                    wait=[prev_x4] + (list(xp_written) if tc == 0 else []),
                    inc=(s_xt, 16))
        e_aff = P.do("vector", "tensor_scalar", lnt[:, lslot * DM:lslot * DM + DM],
                     xtile[:, 0:DM], m2v[:, tc:tc + 1], r2v[:, tc:tc + 1],
                     ALU.subtract, ALU.mult,
                     wait=[e_xt, tile_done4[tc - 2] if tc >= 2 else None])
        prev_x4 = e_aff
        e_col = None
        for g0 in range(0, DCH, 8):
            gn = min(8, DCH - g0)
            grp = (tc * ((DCH + 7) // 8) + g0 // 8) % 2
            e_tp = None
            for j in range(gn):
                c = g0 + j
                e_tp = P.do("tensor", "transpose",
                            pst[:, grp * 1024 + j * 128: grp * 1024 + (j + 1) * 128],
                            lnt[:, lslot * DM + c * 128: lslot * DM + (c + 1) * 128],
                            idb[:, :],
                            wait=[e_aff, half4[grp]] if j == 0 else None,
                            inc=(s_pe, 1) if j == gn - 1 else None)
            e_ev = P.do("vector", "tensor_copy",
                        ap(bufB, g0 * T + tc * 128, [[BIG, 128], [T, gn], [1, 128]]),
                        ap(pst, grp * 1024, [[2048, 128], [128, gn], [1, 128]]),
                        wait=[e_tp])
            half4[grp] = e_ev
            e_col = e_ev
        tile_done4[tc] = e_col
        e_h = None
        for c in range(DCH):
            e_h = P.do("tensor", "matmul", ps[:, 2048:2176],
                       at_sb[:, c * 128:(c + 1) * 128],
                       ap(bufB, c * T + tc * 128, [[BIG, 128], [1, 128]]),
                       start=(c == 0), stop=(c == DCH - 1),
                       wait=[e_col, e_at, rtr_pe_last] if c == 0 else None,
                       inc=(s_pe, 1) if c == DCH - 1 else None)
        e_lg = None
        for c in range(DCH):
            e_lg = P.do("tensor", "matmul", ps[:, 2560:2568],
                        ap(bufB, c * T + tc * 128, [[BIG, 128], [1, 128]]),
                        wr_sb[:, c * 8:(c + 1) * 8],
                        start=(c == 0), stop=(c == DCH - 1),
                        inc=(s_pe, 1) if c == DCH - 1 else None)
        e1 = P.do("vector", "reduce_max", s_m1[:, :], ps[:, 2560:2568], axis=AX,
                  wait=[e_lg])
        e2 = P.do("vector", "tensor_scalar", s_a[:, :], ps[:, 2560:2568],
                  s_m1[:, :], -1.0e30, ALU.is_equal, ALU.mult)
        e3 = P.do("vector", "tensor_tensor", s_a[:, :], s_a[:, :],
                  ps[:, 2560:2568], ALU.add)
        e4 = P.do("vector", "reduce_max", s_m2[:, :], s_a[:, :], axis=AX)
        e5 = P.do("vector", "tensor_tensor", s_dd[:, :], s_m2[:, :], s_m1[:, :],
                  ALU.subtract)
        e6 = P.do("scalar", "activation", s_w1[:, :], s_dd[:, :], AF.Sigmoid,
                  scale=-1.0, wait=[e5])
        e7 = P.do("scalar", "activation", s_w2[:, :], s_dd[:, :], AF.Sigmoid,
                  scale=1.0)
        e8 = P.do("vector", "tensor_scalar", s_b[:, :], ps[:, 2560:2568],
                  s_m1[:, :], s_w1[:, :], ALU.is_equal, ALU.mult, wait=[e7])
        e9 = P.do("vector", "tensor_scalar", s_c[:, :], s_a[:, :],
                  s_m2[:, :], s_w2[:, :], ALU.is_equal, ALU.mult)
        e10 = P.do("vector", "tensor_tensor", s_b[:, :], s_b[:, :], s_c[:, :],
                   ALU.add)
        e11 = P.do("tensor", "transpose", ps[0:8, 2688:2816], s_b[:, :], idf[:, :],
                   wait=[e10], inc=(s_pe, 1))
        e12 = P.do("vector", "tensor_copy", w8T_sb[:, :], ps[0:8, 2688:2816],
                   wait=[e11])
        e13 = P.do("tensor", "matmul", ps[:, 2816:2944], selb[:, :],
                   w8T_sb[:, :], start=True, stop=True,
                   wait=[e12], inc=(s_pe, 1))
        e14 = P.do("scalar", "activation", w16s[:, :], ps[:, 2816:2944], AF.Copy,
                   scale=LORA_SCALING, wait=[e13])
        e15 = P.do("vector", "tensor_tensor", hw_all[:, tc * 128:(tc + 1) * 128],
                   ps[:, 2048:2176], w16s[:, :], ALU.mult,
                   wait=[e14, e_h])
        rtr_pe_last = e15
        e_hw[tc] = e15
    e_wb = [None, None]
    lastuse_wb = [None, None]

    def emit_wb_dma(og):
        slot = og % 2
        e_wb[slot] = P.do("gpsimd", "dma_start",
                          ap(bufA, slot * (DCH * OW), [[BIG, 128], [OW, DCH], [1, OW]]),
                          ap(wbT, og * OW, [[DM, 128], [128 * DM, DCH], [1, OW]]),
                          wait=[lastuse_wb[slot]],
                          inc=(s_wA if slot == 0 else s_wB, 16))

    emit_wb_dma(0)
    if OG > 1:
        emit_wb_dma(1)
    e_bstk = P.do("gpsimd", "dma_start",
                  bstk_sb[:, :], ap(bstk_d, 0, [[DM, 128], [1, DM]]),
                  wait=[tile_done4[TC - 1]], inc=(s_w, 16))
    P.emit("p4a")

    xs_dma = [None] * 4
    xs_user = [None] * 4
    os_war = [None] * 4
    ps_war = [None] * 4
    e_bbog = None
    e_rowuse = None
    first_xsl = True
    for og in range(OG):
        e_r = P.do("sync", "dma_start", rowstg[:, 0:OW],
                   ap(bb_d, og * OW, [[OW, 1], [1, OW]]),
                   wait=[e_rowuse] + (list(xp_written) if first_xsl else []),
                   inc=(s_xt, 16))
        e_b1 = P.do("tensor", "matmul", ps[:, 2048:2048 + OW], ones1[:, :],
                    rowstg[:, 0:OW], start=True, stop=True,
                    wait=[e_r, e_bbog], inc=(s_pe, 1))
        e_rowuse = e_b1
        e_bbog = P.do("vector", "tensor_copy", bbog[:, :], ps[:, 2048:2048 + OW],
                      wait=[e_b1])
        for tc in range(TC):
            j = og * TC + tc
            sl = j % 4
            xs_dma[sl] = P.do("sync", "dma_start",
                              xsl[:, sl * OW:(sl + 1) * OW],
                              ap(xp, tc * 128 * DM + og * OW, [[DM, 128], [1, OW]]),
                              wait=[xs_user[sl]], inc=(s_sl[sl], 16))
            first_xsl = False
            e_mm = None
            for c in range(DCH):
                e_mm = P.do("tensor", "matmul",
                            ps[:, (j % 4) * 512:(j % 4) * 512 + OW],
                            ap(bufB, c * T + tc * 128, [[BIG, 128], [1, 128]]),
                            ap(bufA, (og % 2) * (DCH * OW) + c * OW, [[BIG, 128], [1, OW]]),
                            start=(c == 0), stop=False,
                            wait=([e_wb[og % 2], ps_war[j % 4], e_hw[tc], e_bstk]
                                  if c == 0 else None))
            e_mm = P.do("tensor", "matmul",
                        ps[:, (j % 4) * 512:(j % 4) * 512 + OW],
                        hw_all[:, tc * 128:(tc + 1) * 128],
                        bstk_sb[:, og * OW:(og + 1) * OW],
                        start=False, stop=True, inc=(s_pe, 1))
            lastuse_wb[og % 2] = e_mm
            if og + 2 < OG and tc == TC - 1:
                emit_wb_dma(og + 2)
            e_o1 = P.do("vector", "tensor_tensor",
                        osl[:, sl * OW:(sl + 1) * OW],
                        ps[:, (j % 4) * 512:(j % 4) * 512 + OW],
                        bbog[:, :], ALU.add,
                        wait=[e_mm, os_war[sl], e_bbog])
            ps_war[j % 4] = e_o1
            e_o2 = P.do("vector", "tensor_tensor",
                        osl[:, sl * OW:(sl + 1) * OW],
                        osl[:, sl * OW:(sl + 1) * OW],
                        xsl[:, sl * OW:(sl + 1) * OW], ALU.add,
                        wait=[xs_dma[sl]])
            xs_user[sl] = e_o2
            os_war[sl] = P.do("sync", "dma_start",
                              ap(out_d, tc * 128 * DM + og * OW, [[DM, 128], [1, OW]]),
                              osl[:, sl * OW:(sl + 1) * OW],
                              wait=[e_o2], inc=(s_os[sl], 16))
    P.wait("sync", *[(s, P.count[s]) for s in s_os])
    P.emit("p4b")

    return nc


def _prep_inputs(inputs, NQ, T):
    DM = NQ * 128
    DKV = (NQ // 4) * 128
    g = {k: np.ascontiguousarray(np.asarray(v, dtype=np.float32)) for k, v in inputs.items()}
    x = g["L0_attn_hidden_states"]
    shared = {
        "wqT": np.ascontiguousarray(g["L0_attn_Wq"].T),
        "wkT": np.ascontiguousarray(g["L0_attn_Wk"].T),
        "wvT": np.ascontiguousarray(g["L0_attn_Wv"].T),
        "bv": g["L0_attn_bv"].reshape(1, DKV),
        "woT": np.ascontiguousarray(g["L0_attn_Wo"].T),
        "wbT": np.ascontiguousarray(g["L0_ffn_W_base"].T),
        "bb": g["L0_ffn_b_base"].reshape(1, DM),
        "wrT": np.ascontiguousarray(g["L0_ffn_W_router"].T),
        "at": np.ascontiguousarray(g["L0_ffn_A"].reshape(128, DM).T),
        "bstk": np.ascontiguousarray(g["L0_ffn_B"].transpose(0, 2, 1).reshape(128, DM)),
    }
    in_maps = []
    for c in range(8):
        m = dict(shared)
        m["x"] = np.ascontiguousarray(x[c // 4, (c % 4) * T:(c % 4 + 1) * T, :])
        in_maps.append(m)
    return in_maps


def kernel(**inputs):
    NQ, T = 32, 1024
    nc = build(NQ, T)
    in_maps = _prep_inputs(inputs, NQ, T)
    res = run_bass_kernel_spmd(nc, in_maps, core_ids=list(range(8)))
    DM = NQ * 128
    out = np.empty((2, 4 * T, DM), dtype=np.float32)
    for c in range(8):
        out[c // 4, (c % 4) * T:(c % 4 + 1) * T, :] = res.results[c]["out"]
    return out


# version 19 (final)
# speedup vs baseline: 1.1015x; 1.1015x over previous
import numpy as np
import ml_dtypes
import concourse.bass as bass
import concourse.mybir as mybir
from concourse.bass_utils import run_bass_kernel_spmd

F32 = mybir.dt.float32
BF16 = mybir.dt.bfloat16
AF = mybir.ActivationFunctionType
ALU = mybir.AluOpType
AX = mybir.AxisListType.X

QK_SCALE = float(2.0 ** -3.5)
LN_EPS = 1e-5
LORA_SCALING = 2.0


class Plan:
    ENGINES = ("tensor", "vector", "scalar", "gpsimd", "sync")

    def __init__(self, nc):
        self.nc = nc
        self.q = {e: [] for e in self.ENGINES}
        self.count = {}
        self.auto = {}
        self.last = {e: None for e in self.ENGINES}

    def sem(self, name):
        h = self.nc.alloc_semaphore(name)
        self.count[h] = 0
        return h

    def set_auto(self, eng, sem):
        self.auto[eng] = sem

    def do(self, eng, method, *args, wait=None, inc=None, chain=True, **kwargs):
        waits = [w for w in (wait or ()) if w is not None]
        if eng in self.auto:
            if chain and self.last[eng] is not None:
                waits.append(self.last[eng])
            if inc is None:
                inc = (self.auto[eng], 1)
        best = {}
        for (s, v) in waits:
            if best.get(s, -1) < v:
                best[s] = v
        waits = tuple(best.items())
        tgt = None
        if inc is not None:
            s, n = inc
            self.count[s] += n
            tgt = (s, self.count[s])
            if eng in self.auto and s == self.auto[eng]:
                self.last[eng] = tgt

        def fn(E, method=method, args=args, kwargs=kwargs, waits=waits, inc=inc):
            for (s, v) in waits:
                E.wait_ge(s, v)
            ins = getattr(E, method)(*args, **kwargs)
            if inc is not None:
                ins.then_inc(inc[0], inc[1])
            return ins

        self.q[eng].append(fn)
        return tgt

    def wait(self, eng, *events):
        evs = tuple(e for e in events if e is not None)

        def fn(E, evs=evs):
            for (s, v) in evs:
                E.wait_ge(s, v)

        self.q[eng].append(fn)

    def emit(self, name=None):
        nc = self.nc
        with nc.Block(name) as block:
            for eng in self.ENGINES:
                queue = self.q[eng]
                if not queue:
                    continue
                getattr(block, eng)(lambda E, queue=queue: [fn(E) for fn in queue])
        self.q = {e: [] for e in self.ENGINES}


def ap(t, offset, dims):
    return bass.AP(t, offset, [list(d) for d in dims])


def build(NQ, T, dbg=False):
    NKV = NQ // 4
    DM = NQ * 128
    DKV = NKV * 128
    S = 4 * T
    CH = S // 128
    TC = T // 128
    DCH = DM // 128
    QC = [(o, min(512, T - o)) for o in range(0, T, 512)]
    NQC = len(QC)
    OW = min(512, DM)
    OG = DM // OW
    NB = DM // 512

    nc = bass.Bass("TRN2", target_bir_lowering=False)

    x_d = nc.dram_tensor("x", [T, DM], F32, kind="ExternalInput")
    wqT = nc.dram_tensor("wqT", [DM, DM], F32, kind="ExternalInput")
    wkT = nc.dram_tensor("wkT", [DM, DKV], F32, kind="ExternalInput")
    wvT = nc.dram_tensor("wvT", [DM, DKV], F32, kind="ExternalInput")
    bv_d = nc.dram_tensor("bv", [1, DKV], F32, kind="ExternalInput")
    woT = nc.dram_tensor("woT", [DM, DM], F32, kind="ExternalInput")
    wbT = nc.dram_tensor("wbT", [DM, DM], F32, kind="ExternalInput")
    bb_d = nc.dram_tensor("bb", [1, DM], F32, kind="ExternalInput")
    wrT = nc.dram_tensor("wrT", [DM, 8], F32, kind="ExternalInput")
    at_d = nc.dram_tensor("at", [DM, 128], F32, kind="ExternalInput")
    bstk_d = nc.dram_tensor("bstk", [128, DM], F32, kind="ExternalInput")
    out_d = nc.dram_tensor("out", [T, DM], F32, kind="ExternalOutput")

    kin = nc.dram_tensor("kin", [DKV, T], BF16)
    vin = nc.dram_tensor("vin", [T, DKV], BF16)
    kall = nc.dram_tensor("kall", [4 * DKV, T], BF16)
    vall = nc.dram_tensor("vall", [4 * T, DKV], BF16)
    xp = nc.dram_tensor("xp", [T, DM], F32)

    ident_bf = nc.inline_tensor(np.eye(128, dtype=ml_dtypes.bfloat16), name="ident_bf")
    ident_f = nc.inline_tensor(np.eye(128, dtype=np.float32), name="ident_f")
    allones = nc.inline_tensor(np.ones((128, 128), dtype=ml_dtypes.bfloat16), name="allones")
    allonesf = nc.inline_tensor(np.ones((128, 128), dtype=np.float32), name="allonesf")
    sel_np = np.zeros((8, 128), dtype=np.float32)
    for e in range(8):
        sel_np[e, e * 16:(e + 1) * 16] = 1.0
    sel_c = nc.inline_tensor(sel_np, name="sel")
    onescol = nc.inline_tensor(np.ones((1, 128), dtype=np.float32), name="onescol")

    A = nc.alloc_sbuf_tensor
    idb = A("idb", [128, 128], BF16)
    idf = A("idf", [128, 128], F32)
    onesb = A("onesb", [128, 128], BF16)
    onesf = A("onesf", [128, 128], F32)
    selb = A("selb", [8, 128], F32)
    ones1 = A("ones1", [1, 128], F32)
    bv_col = A("bv_col", [128, NKV], F32)
    wr_sb = A("wr_sb", [128, DCH * 8], BF16)
    hw_all = A("hw_all", [128, TC * 128], BF16)
    rowstg = A("rowstg", [1, 512], F32)
    bbog = A("bbog", [128, OW], F32)
    recip = A("recip", [128, T], F32)
    accP = A("accP", [128, T], F32)
    expb = A("expb", [128, 4 * T], BF16)
    _expb_addr = nc.lookup_mloc(expb).addr
    vfm_sb = nc.alloc_sbuf_tensor_at("vfm_sb", [128, T], BF16, offset=_expb_addr)
    ktile_sb = nc.alloc_sbuf_tensor_at("ktile_sb", [128, T], BF16, offset=_expb_addr + T * 2)
    w16s = A("w16s", [128, 128], F32)
    w8T_sb = A("w8T_sb", [8, 128], F32)
    bn6 = A("bn6", [128, 6 * NB], F32)
    bn6b = A("bn6b", [128, TC * OG * 6], F32)
    bn2 = A("bn2", [128, 2], F32)
    m2v = A("m2v", [128, TC], F32)
    r2v = A("r2v", [128, TC], F32)
    s_a = A("s_a", [128, 8], F32)
    s_b = A("s_b", [128, 8], F32)
    s_c = A("s_c", [128, 8], F32)
    s_m1 = A("s_m1", [128, 1], F32)
    s_m2 = A("s_m2", [128, 1], F32)
    s_dd = A("s_dd", [128, 1], F32)
    s_w1 = A("s_w1", [128, 1], F32)
    s_w2 = A("s_w2", [128, 1], F32)
    s_v1 = A("s_v1", [128, 1], F32)
    s_r1 = A("s_r1", [128, 1], F32)
    eps_sb = A("eps_sb", [128, 1], F32)

    BIG = max(NQ * T, 2 * DCH * OW)
    bufA = A("bufA", [128, BIG], BF16)
    bufB = A("bufB", [128, BIG], BF16)
    R3B = max(DM * 4, S * 2 * 2)
    xtile = A("xtile", [128, R3B // 4], F32)
    r3_addr = nc.lookup_mloc(xtile).addr
    kbuf = nc.alloc_sbuf_tensor_at("kbuf", [128, S], BF16, offset=r3_addr)
    vbuf = nc.alloc_sbuf_tensor_at("vbuf", [128, S], BF16, offset=r3_addr + S * 2)
    R4B = max(2 * DM * 2, 8 * OW * 4)
    wqbuf = A("wqbuf", [128, R4B // 2], BF16)
    r4_addr = nc.lookup_mloc(wqbuf).addr
    kbuf2 = nc.alloc_sbuf_tensor_at("kbuf2", [128, S], BF16, offset=r4_addr)
    vbuf2 = nc.alloc_sbuf_tensor_at("vbuf2", [128, S], BF16, offset=r4_addr + S * 2)
    xsl = nc.alloc_sbuf_tensor_at("xsl", [128, 4 * OW], F32, offset=r4_addr)
    osl = nc.alloc_sbuf_tensor_at("osl", [128, 4 * OW], F32, offset=r4_addr + 4 * OW * 4)
    at_sb = nc.alloc_sbuf_tensor_at("at_sb", [128, DCH * 128], BF16, offset=r4_addr)
    R5B = max(2 * DM * 2, TC * DKV * 2, DM * 2)
    lnt = A("lnt", [128, R5B // 2], BF16)
    r5_addr = nc.lookup_mloc(lnt).addr
    vtm = nc.alloc_sbuf_tensor_at("vtm", [128, TC * DKV], BF16, offset=r5_addr)
    bstk_sb = nc.alloc_sbuf_tensor_at("bstk_sb", [128, DM], BF16, offset=r5_addr)

    ps = nc.alloc_psum_tensor("ps", [128, 3072], F32)
    pst = nc.alloc_psum_tensor("pst", [128, 2048], BF16)

    P = Plan(nc)
    s_w = P.sem("s_w")
    s_wA = P.sem("s_wA")
    s_wB = P.sem("s_wB")
    s_xt = P.sem("s_xt")
    s_kb = [P.sem("s_kb0"), P.sem("s_kb1")]
    s_vb = [P.sem("s_vb0"), P.sem("s_vb1")]
    s_sl = [P.sem(f"s_sl{i}") for i in range(4)]
    s_os = [P.sem(f"s_os{i}") for i in range(4)]
    s_pe = P.sem("s_pe")
    s_act = P.sem("s_act")
    s_dve = P.sem("s_dve")
    s_cc = P.sem("s_cc")
    s_kvw = P.sem("s_kvw")
    P.set_auto("vector", s_dve)
    P.set_auto("scalar", s_act)

    for dst, src in ((idb, ident_bf), (idf, ident_f), (onesb, allones),
                     (onesf, allonesf), (selb, sel_c), (ones1, onescol)):
        P.do("gpsimd", "dma_start", dst[:, :], src.ap(), inc=(s_w, 16))
    P.do("gpsimd", "dma_start",
         bv_col[:, :], ap(bv_d, 0, [[1, 128], [128, NKV], [1, 1]]),
         allow_slow_non_contiguous=True, inc=(s_w, 16))
    e_consts = P.do("gpsimd", "dma_start",
                    wr_sb[:, :], ap(wrT, 0, [[8, 128], [128 * 8, DCH], [1, 8]]),
                    inc=(s_w, 16))
    P.do("vector", "memset", eps_sb[:, :], LN_EPS)
    P.wait("gpsimd", e_consts)
    blocks = [("q", i, wqT, DM) for i in range(NQ)] + \
             [("k", i, wkT, DKV) for i in range(NKV)] + \
             [("v", i, wvT, DKV) for i in range(NKV)]
    e_blk = [None] * len(blocks)
    last_use = [None] * len(blocks)

    def emit_blk_dma(b):
        kind, i, wt, wcols = blocks[b]
        slot = b % 2
        e_blk[b] = P.do("gpsimd", "dma_start",
                        ap(wqbuf, slot * DM, [[R4B // 2, 128], [128, DCH], [1, 128]]),
                        ap(wt, i * 128, [[wcols, 128], [128 * wcols, DCH], [1, 128]]),
                        wait=[last_use[b - 2] if b >= 2 else None],
                        inc=(s_wA if slot == 0 else s_wB, 16))

    emit_blk_dma(0)
    if len(blocks) > 1:
        emit_blk_dma(1)
    P.emit("consts")

    tile_done = [None] * TC
    half_evac = [None, None]
    prev_x = None
    for tc in range(TC):
        lslot = tc % 2
        e_xt = P.do("sync", "dma_start", xtile[:, 0:DM],
                    ap(x_d, tc * 128 * DM, [[DM, 128], [1, DM]]),
                    wait=[prev_x], inc=(s_xt, 16))
        eb = None
        for k in range(NB):
            eb = P.do("vector", "bn_stats", bn6[:, k * 6:(k + 1) * 6],
                      xtile[:, k * 512:(k + 1) * 512],
                      wait=[e_xt] if k == 0 else None)
        e_ag = P.do("vector", "bn_aggr", bn2[:, :], bn6[:, :])
        e_sq = P.do("scalar", "activation", s_v1[:, :], bn2[:, 1:2], AF.Sqrt,
                    bias=eps_sb[:, :], wait=[e_ag])
        e_rc = P.do("vector", "reciprocal", s_r1[:, :], s_v1[:, :], wait=[e_sq])
        e_aff = P.do("vector", "tensor_scalar", lnt[:, lslot * DM:lslot * DM + DM],
                     xtile[:, 0:DM], bn2[:, 0:1], s_r1[:, :], ALU.subtract, ALU.mult,
                     wait=[tile_done[tc - 2] if tc >= 2 else None])
        prev_x = e_aff
        last_ev = None
        for g0 in range(0, DCH, 8):
            gn = min(8, DCH - g0)
            grp = (tc * ((DCH + 7) // 8) + g0 // 8) % 2
            e_tp = None
            for j in range(gn):
                c = g0 + j
                e_tp = P.do("tensor", "transpose",
                            pst[:, grp * 1024 + j * 128: grp * 1024 + (j + 1) * 128],
                            lnt[:, lslot * DM + c * 128: lslot * DM + (c + 1) * 128],
                            idb[:, :],
                            wait=[e_aff, half_evac[grp]] if j == 0 else None,
                            inc=(s_pe, 1) if j == gn - 1 else None)
            e_ev = P.do("vector", "tensor_copy",
                        ap(bufA, g0 * T + tc * 128, [[BIG, 128], [T, gn], [1, 128]]),
                        ap(pst, grp * 1024, [[2048, 128], [128, gn], [1, 128]]),
                        wait=[e_tp])
            half_evac[grp] = e_ev
            last_ev = e_ev
        tile_done[tc] = last_ev
    e_ln_done = tile_done[TC - 1]

    ps_evac = [None, None, None, None]
    k_dma_prev = None
    v_tp_done = None
    vhalf_evac = [e for e in half_evac]
    vt_dma = [None] * TC
    for b, (kind, i, wt, wcols) in enumerate(blocks):
        tile_evs = []
        base = (b % 2) * 2
        e_mm = None
        for c in range(DCH):
            for qi, (qo, qw) in enumerate(QC):
                slot = base + qi
                w0 = None
                if c == 0:
                    w0 = [e_blk[b], e_ln_done, ps_evac[slot]] if qi == 0 else [ps_evac[slot]]
                e_mm = P.do("tensor", "matmul", ps[:, slot * 512: slot * 512 + qw],
                            ap(wqbuf, (b % 2) * DM + c * 128, [[R4B // 2, 128], [1, 128]]),
                            ap(bufA, c * T + qo, [[BIG, 128], [1, qw]]),
                            start=(c == 0), stop=(c == DCH - 1),
                            wait=w0,
                            inc=(s_pe, 1) if (c == DCH - 1 and qi == NQC - 1) else None)
        last_use[b] = e_mm
        for qi, (qo, qw) in enumerate(QC):
            slot = base + qi
            if kind == "q":
                e_ev = P.do("scalar", "copy",
                            ap(bufB, i * T + qo, [[BIG, 128], [1, qw]]),
                            ps[:, slot * 512: slot * 512 + qw],
                            wait=[e_mm])
            elif kind == "k":
                e_ev = P.do("scalar", "copy",
                            ap(ktile_sb, qo, [[T, 128], [1, qw]]),
                            ps[:, slot * 512: slot * 512 + qw],
                            wait=[e_mm, k_dma_prev if qo == 0 else None])
            else:
                e_ev = P.do("vector", "tensor_scalar_add",
                            ap(vfm_sb, qo, [[T, 128], [1, qw]]),
                            ps[:, slot * 512: slot * 512 + qw],
                            bv_col[:, i:i + 1],
                            wait=[e_mm, v_tp_done if qo == 0 else None])
            ps_evac[slot] = e_ev
            tile_evs.append(e_ev)
        if b + 2 < len(blocks):
            emit_blk_dma(b + 2)
        if kind == "k":
            k_dma_prev = P.do("sync", "dma_start",
                              ap(kin, i * 128 * T, [[T, 128], [1, T]]),
                              ktile_sb[:, :], wait=[tile_evs[-1]], inc=(s_kvw, 16))
        elif kind == "v":
            for tcsub in range(TC):
                grp = tcsub % 2
                w0 = [tile_evs[-1], vhalf_evac[grp]]
                if i == 0 and tcsub < 2:
                    w0.append(e_ln_done)
                e_tp = P.do("tensor", "transpose",
                            pst[:, grp * 1024: grp * 1024 + 128],
                            vfm_sb[:, tcsub * 128:(tcsub + 1) * 128], idb[:, :],
                            wait=w0, inc=(s_pe, 1))
                v_tp_done = e_tp
                e_ev2 = P.do("vector", "tensor_copy",
                             ap(vtm, tcsub * DKV + i * 128, [[TC * DKV, 128], [1, 128]]),
                             pst[:, grp * 1024: grp * 1024 + 128],
                             wait=[e_tp, vt_dma[tcsub] if i == NKV - 1 else None])
                vhalf_evac[grp] = e_ev2
                if i == NKV - 1:
                    vt_dma[tcsub] = P.do("sync", "dma_start",
                                         ap(vin, tcsub * 128 * DKV, [[DKV, 128], [1, DKV]]),
                                         vtm[:, tcsub * DKV:(tcsub + 1) * DKV],
                                         wait=[e_ev2], inc=(s_kvw, 16))

    P.wait("gpsimd", (s_kvw, P.count[s_kvw]))
    e_cck = P.do("gpsimd", "collective_compute", "AllGather", ALU.bypass,
                 [[0, 1, 2, 3], [4, 5, 6, 7]], [kin.ap()], [kall.ap()], inc=(s_cc, 1))
    e_ccv = P.do("gpsimd", "collective_compute", "AllGather", ALU.bypass,
                 [[0, 1, 2, 3], [4, 5, 6, 7]], [vin.ap()], [vall.ap()], inc=(s_cc, 1))
    P.emit("p1")

    s_pool = P.sem("s_pool")
    P.set_auto("gpsimd", s_pool)
    exp_ev = {}
    acc_ev = {}
    gc = 0
    e_kb = e_vb = None
    prev_ctx_evac = None
    prev_recip = None
    prev_e_sm = None
    pend_pe = []

    def flush_pe():
        while pend_pe:
            pend_pe.pop(0)()

    last_score = None
    kvbufs = [(kbuf, vbuf), (kbuf2, vbuf2)]
    e_kv_slot = [None, None]

    def emit_kv_dma(kv, wait_k, wait_v):
        kb, vb = kvbufs[kv % 2]
        ek = P.do("sync", "dma_start", kb[:, :],
                  ap(kall, kv * 128 * T, [[T, 128], [DKV * T, 4], [1, T]]),
                  wait=[wait_k, e_cck, e_ccv], inc=(s_kb[kv % 2], 16))
        ev = P.do("sync", "dma_start", vb[:, :],
                  ap(vall, kv * 128, [[DKV, 128], [128 * DKV, CH], [1, 128]]),
                  wait=[wait_v], inc=(s_vb[kv % 2], 16))
        e_kv_slot[kv % 2] = (ek, ev)

    emit_kv_dma(0, None, None)
    for h in range(NQ):
        kvh = h // 4
        kbuf_c, vbuf_c = kvbufs[kvh % 2]
        if h % 4 == 0:
            flush_pe()
            if kvh + 1 < NKV:
                emit_kv_dma(kvh + 1, last_score, (s_pe, P.count[s_pe]))
            e_kb, e_vb = e_kv_slot[kvh % 2]
        accV_first = True
        accP_first = True
        for c in range(CH):
            i = gc
            sl = (i % 2) * T
            wsc = []
            if c == 0:
                wsc += [e_kb, e_vb]
            if c == 1:
                wsc.append(prev_recip)
            if c == 2 and gc >= CH:
                pass
            e_sc = None
            for (qo, qw) in QC:
                e_sc = P.do("tensor", "matmul",
                            ps[:, sl + qo: sl + qo + qw],
                            kbuf_c[:, c * 128:(c + 1) * 128],
                            ap(bufB, h * T + qo, [[BIG, 128], [1, qw]]),
                            start=True, stop=True,
                            wait=(wsc + [exp_ev.get(i - 2)]) if qo == 0 else None,
                            inc=(s_pe, 1) if qo == QC[-1][0] else None)
            last_score = e_sc
            exp_ev[i] = P.do("scalar", "activation",
                             expb[:, (i % 4) * T:(i % 4) * T + T],
                             ps[:, sl: sl + T],
                             AF.Exp, scale=QK_SCALE,
                             wait=[e_sc, acc_ev.get(i - 4)], chain=False)
            flush_pe()

            def mk_pv(c=c, i=i, h=h, first_wait=prev_ctx_evac, vbuf_c=vbuf_c):
                def emit():
                    for (qo, qw) in QC:
                        P.do("tensor", "matmul",
                             ps[:, 2048 + qo: 2048 + qo + qw],
                             vbuf_c[:, c * 128:(c + 1) * 128],
                             expb[:, (i % 4) * T + qo:(i % 4) * T + qo + qw],
                             start=(c == 0), stop=(c == CH - 1),
                             wait=([exp_ev[i], first_wait if c == 0 else None]
                                   if qo == 0 else None),
                             inc=(s_pe, 1) if (c == CH - 1 and qo == QC[-1][0]) else None)
                return emit
            pend_pe.append(mk_pv())
            if c % 4 == 1:
                if accP_first:
                    P.do("gpsimd", "memset", accP[:, 0:T], 0.0,
                         wait=[prev_e_sm])
                    acc_ev[i] = P.do("gpsimd", "tensor_tensor", accP[:, 0:T],
                                     accP[:, 0:T],
                                     expb[:, (i % 4) * T:(i % 4) * T + T], ALU.add,
                                     wait=[exp_ev[i]])
                    accP_first = False
                else:
                    acc_ev[i] = P.do("gpsimd", "tensor_tensor", accP[:, 0:T],
                                     accP[:, 0:T],
                                     expb[:, (i % 4) * T:(i % 4) * T + T], ALU.add,
                                     wait=[exp_ev[i]])
            else:
                if accV_first:
                    P.do("vector", "memset", recip[:, 0:T], 0.0)
                    acc_ev[i] = P.do("vector", "tensor_tensor", recip[:, 0:T],
                                     recip[:, 0:T],
                                     expb[:, (i % 4) * T:(i % 4) * T + T], ALU.add,
                                     wait=[exp_ev[i]])
                    accV_first = False
                else:
                    acc_ev[i] = P.do("vector", "tensor_tensor", recip[:, 0:T],
                                     recip[:, 0:T],
                                     expb[:, (i % 4) * T:(i % 4) * T + T], ALU.add,
                                     wait=[exp_ev[i]])
            gc += 1
        eV = acc_ev[gc - 1]
        eP = acc_ev[gc - 3]
        e31, e30 = exp_ev[gc - 1], exp_ev[gc - 2]

        def mk_sums(eV=eV, eP=eP, e31=e31, e30=e30, holder=None):
            def emit():
                e_sm = None
                for (qo, qw) in QC:
                    P.do("tensor", "matmul", ps[:, 1024 + qo: 1024 + qo + qw],
                         onesf[:, :], recip[:, qo: qo + qw], start=True, stop=False,
                         wait=[eV, eP, e31, e30] if qo == 0 else None)
                    e_sm = P.do("tensor", "matmul", ps[:, 1024 + qo: 1024 + qo + qw],
                                onesf[:, :], accP[:, qo: qo + qw], start=False,
                                stop=True, inc=(s_pe, 1))
                holder["e"] = e_sm
            return emit

        sums_holder = {}
        pend_pe.append(mk_sums(holder=sums_holder))

        def v_tail(h=h, sums_holder=sums_holder):
            e_sm = sums_holder["e"]
            e_ln = P.do("scalar", "activation", accP[:, 0:T],
                        ps[:, 1024:1024 + T], AF.Ln, wait=[e_sm], chain=False)
            e_rcp = P.do("scalar", "activation", recip[:, 0:T],
                         accP[:, 0:T], AF.Exp, scale=-1.0, wait=[e_ln], chain=False)
            e_evac = P.do("vector", "tensor_tensor",
                          ap(bufA, h * T, [[BIG, 128], [1, T]]),
                          ps[:, 2048:2048 + T], recip[:, 0:T], ALU.mult,
                          wait=[e_rcp, (s_pe, P.count[s_pe])])
            return e_ln, e_rcp, e_evac

        def mk_vtail(fn=v_tail, sums_holder=sums_holder):
            def emit():
                nonlocal prev_recip, prev_ctx_evac, prev_e_sm
                e_ln, e_rcp, e_evac = fn()
                prev_recip = e_ln
                prev_e_sm = e_rcp
                prev_ctx_evac = e_evac
            return emit
        pend_pe.append(mk_vtail())
    flush_pe()
    P.emit("p2")

    e_wo = [None, None]
    lastuse_wo = [None, None]

    def emit_wo_dma(og):
        slot = og % 2
        e_wo[slot] = P.do("gpsimd", "dma_start",
                          ap(bufB, slot * (DCH * OW), [[BIG, 128], [OW, DCH], [1, OW]]),
                          ap(woT, og * OW, [[DM, 128], [128 * DM, DCH], [1, OW]]),
                          wait=[lastuse_wo[slot]],
                          inc=(s_wA if slot == 0 else s_wB, 16))

    emit_wo_dma(0)
    if OG > 1:
        emit_wo_dma(1)
    xs_dma = [None] * 4
    xs_user = [None] * 4
    os_war = [None] * 4
    ps_war = [None] * 4
    bn_last = None
    for og in range(OG):
        for tc in range(TC):
            j = og * TC + tc
            sl = j % 4
            xs_dma[sl] = P.do("sync", "dma_start",
                              xsl[:, sl * OW:(sl + 1) * OW],
                              ap(x_d, tc * 128 * DM + og * OW, [[DM, 128], [1, OW]]),
                              wait=[xs_user[sl]], inc=(s_sl[sl], 16))
            e_mm = None
            for c in range(DCH):
                e_mm = P.do("tensor", "matmul",
                            ps[:, (j % 4) * 512:(j % 4) * 512 + OW],
                            ap(bufA, c * T + tc * 128, [[BIG, 128], [1, 128]]),
                            ap(bufB, (og % 2) * (DCH * OW) + c * OW, [[BIG, 128], [1, OW]]),
                            start=(c == 0), stop=(c == DCH - 1),
                            wait=[e_wo[og % 2], ps_war[j % 4]] if c == 0 else None,
                            inc=(s_pe, 1) if c == DCH - 1 else None)
            lastuse_wo[og % 2] = e_mm
            if og + 2 < OG and tc == TC - 1:
                emit_wo_dma(og + 2)
            e_xp = P.do("vector", "tensor_tensor",
                        osl[:, sl * OW:(sl + 1) * OW],
                        ps[:, (j % 4) * 512:(j % 4) * 512 + OW],
                        xsl[:, sl * OW:(sl + 1) * OW], ALU.add,
                        wait=[e_mm, xs_dma[sl], os_war[sl]])
            ps_war[j % 4] = e_xp
            xs_user[sl] = e_xp
            jj = tc * OG + og
            e_bn = P.do("vector", "bn_stats",
                        bn6b[:, jj * 6:(jj + 1) * 6],
                        osl[:, sl * OW:(sl + 1) * OW])
            bn_last = e_bn
            os_war[sl] = P.do("scalar", "dma_start",
                              ap(xp, tc * 128 * DM + og * OW, [[DM, 128], [1, OW]]),
                              osl[:, sl * OW:(sl + 1) * OW],
                              wait=[e_bn], inc=(s_os[sl], 16), chain=False)
    prev_sq = None
    for tc in range(TC):
        e_ag = P.do("vector", "bn_aggr", bn2[:, :],
                    bn6b[:, tc * OG * 6:(tc + 1) * OG * 6],
                    wait=[bn_last, prev_sq])
        e_mv = P.do("vector", "tensor_copy", m2v[:, tc:tc + 1], bn2[:, 0:1])
        e_sq = P.do("scalar", "activation", s_v1[:, :], bn2[:, 1:2], AF.Sqrt,
                    bias=eps_sb[:, :], wait=[e_mv])
        prev_sq = e_sq
        P.do("vector", "reciprocal", r2v[:, tc:tc + 1], s_v1[:, :], wait=[e_sq])
    xp_written = tuple((s, P.count[s]) for s in s_os)
    P.emit("p3")

    e_at = P.do("gpsimd", "dma_start",
                at_sb[:, :], ap(at_d, 0, [[128, 128], [128 * 128, DCH], [1, 128]]),
                inc=(s_w, 16))
    prev_x4 = None
    half4 = [None, None]
    tile_done4 = [None] * TC
    rtr_pe_last = None
    e_hw = [None] * TC
    for tc in range(TC):
        lslot = tc % 2
        e_xt = P.do("sync", "dma_start", xtile[:, 0:DM],
                    ap(xp, tc * 128 * DM, [[DM, 128], [1, DM]]),
                    wait=[prev_x4] + (list(xp_written) if tc == 0 else []),
                    inc=(s_xt, 16))
        e_aff = P.do("vector", "tensor_scalar", lnt[:, lslot * DM:lslot * DM + DM],
                     xtile[:, 0:DM], m2v[:, tc:tc + 1], r2v[:, tc:tc + 1],
                     ALU.subtract, ALU.mult,
                     wait=[e_xt, tile_done4[tc - 2] if tc >= 2 else None])
        prev_x4 = e_aff
        e_col = None
        for g0 in range(0, DCH, 8):
            gn = min(8, DCH - g0)
            grp = (tc * ((DCH + 7) // 8) + g0 // 8) % 2
            e_tp = None
            for j in range(gn):
                c = g0 + j
                e_tp = P.do("tensor", "transpose",
                            pst[:, grp * 1024 + j * 128: grp * 1024 + (j + 1) * 128],
                            lnt[:, lslot * DM + c * 128: lslot * DM + (c + 1) * 128],
                            idb[:, :],
                            wait=[e_aff, half4[grp]] if j == 0 else None,
                            inc=(s_pe, 1) if j == gn - 1 else None)
            e_ev = P.do("vector", "tensor_copy",
                        ap(bufB, g0 * T + tc * 128, [[BIG, 128], [T, gn], [1, 128]]),
                        ap(pst, grp * 1024, [[2048, 128], [128, gn], [1, 128]]),
                        wait=[e_tp])
            half4[grp] = e_ev
            e_col = e_ev
        tile_done4[tc] = e_col
        e_h = None
        for c in range(DCH):
            e_h = P.do("tensor", "matmul", ps[:, 2048:2176],
                       at_sb[:, c * 128:(c + 1) * 128],
                       ap(bufB, c * T + tc * 128, [[BIG, 128], [1, 128]]),
                       start=(c == 0), stop=(c == DCH - 1),
                       wait=[e_col, e_at, rtr_pe_last] if c == 0 else None,
                       inc=(s_pe, 1) if c == DCH - 1 else None)
        e_lg = None
        for c in range(DCH):
            e_lg = P.do("tensor", "matmul", ps[:, 2560:2568],
                        ap(bufB, c * T + tc * 128, [[BIG, 128], [1, 128]]),
                        wr_sb[:, c * 8:(c + 1) * 8],
                        start=(c == 0), stop=(c == DCH - 1),
                        inc=(s_pe, 1) if c == DCH - 1 else None)
        e1 = P.do("vector", "reduce_max", s_m1[:, :], ps[:, 2560:2568], axis=AX,
                  wait=[e_lg])
        e2 = P.do("vector", "tensor_scalar", s_a[:, :], ps[:, 2560:2568],
                  s_m1[:, :], -1.0e30, ALU.is_equal, ALU.mult)
        e3 = P.do("vector", "tensor_tensor", s_a[:, :], s_a[:, :],
                  ps[:, 2560:2568], ALU.add)
        e4 = P.do("vector", "reduce_max", s_m2[:, :], s_a[:, :], axis=AX)
        e5 = P.do("vector", "tensor_tensor", s_dd[:, :], s_m2[:, :], s_m1[:, :],
                  ALU.subtract)
        e6 = P.do("scalar", "activation", s_w1[:, :], s_dd[:, :], AF.Sigmoid,
                  scale=-1.0, wait=[e5])
        e7 = P.do("scalar", "activation", s_w2[:, :], s_dd[:, :], AF.Sigmoid,
                  scale=1.0)
        e8 = P.do("vector", "tensor_scalar", s_b[:, :], ps[:, 2560:2568],
                  s_m1[:, :], s_w1[:, :], ALU.is_equal, ALU.mult, wait=[e7])
        e9 = P.do("vector", "tensor_scalar", s_c[:, :], s_a[:, :],
                  s_m2[:, :], s_w2[:, :], ALU.is_equal, ALU.mult)
        e10 = P.do("vector", "tensor_tensor", s_b[:, :], s_b[:, :], s_c[:, :],
                   ALU.add)
        e11 = P.do("tensor", "transpose", ps[0:8, 2688:2816], s_b[:, :], idf[:, :],
                   wait=[e10], inc=(s_pe, 1))
        e12 = P.do("vector", "tensor_copy", w8T_sb[:, :], ps[0:8, 2688:2816],
                   wait=[e11])
        e13 = P.do("tensor", "matmul", ps[:, 2816:2944], selb[:, :],
                   w8T_sb[:, :], start=True, stop=True,
                   wait=[e12], inc=(s_pe, 1))
        e14 = P.do("scalar", "activation", w16s[:, :], ps[:, 2816:2944], AF.Copy,
                   scale=LORA_SCALING, wait=[e13])
        e15 = P.do("vector", "tensor_tensor", hw_all[:, tc * 128:(tc + 1) * 128],
                   ps[:, 2048:2176], w16s[:, :], ALU.mult,
                   wait=[e14, e_h])
        rtr_pe_last = e15
        e_hw[tc] = e15
    P.emit("p4a")

    e_bstk = P.do("gpsimd", "dma_start",
                  bstk_sb[:, :], ap(bstk_d, 0, [[DM, 128], [1, DM]]), inc=(s_w, 16))
    e_wb = [None, None]
    lastuse_wb = [None, None]

    def emit_wb_dma(og):
        slot = og % 2
        e_wb[slot] = P.do("gpsimd", "dma_start",
                          ap(bufA, slot * (DCH * OW), [[BIG, 128], [OW, DCH], [1, OW]]),
                          ap(wbT, og * OW, [[DM, 128], [128 * DM, DCH], [1, OW]]),
                          wait=[lastuse_wb[slot]],
                          inc=(s_wA if slot == 0 else s_wB, 16))

    emit_wb_dma(0)
    if OG > 1:
        emit_wb_dma(1)
    xs_dma = [None] * 4
    xs_user = [None] * 4
    os_war = [None] * 4
    ps_war = [None] * 4
    e_bbog = None
    e_rowuse = None
    first_xsl = True
    for og in range(OG):
        e_r = P.do("sync", "dma_start", rowstg[:, 0:OW],
                   ap(bb_d, og * OW, [[OW, 1], [1, OW]]),
                   wait=[e_rowuse] + (list(xp_written) if first_xsl else []),
                   inc=(s_xt, 16))
        e_b1 = P.do("tensor", "matmul", ps[:, 2048:2048 + OW], ones1[:, :],
                    rowstg[:, 0:OW], start=True, stop=True,
                    wait=[e_r, e_bbog], inc=(s_pe, 1))
        e_rowuse = e_b1
        e_bbog = P.do("vector", "tensor_copy", bbog[:, :], ps[:, 2048:2048 + OW],
                      wait=[e_b1])
        for tc in range(TC):
            j = og * TC + tc
            sl = j % 4
            xs_dma[sl] = P.do("sync", "dma_start",
                              xsl[:, sl * OW:(sl + 1) * OW],
                              ap(xp, tc * 128 * DM + og * OW, [[DM, 128], [1, OW]]),
                              wait=[xs_user[sl]], inc=(s_sl[sl], 16))
            first_xsl = False
            e_mm = None
            for c in range(DCH):
                e_mm = P.do("tensor", "matmul",
                            ps[:, (j % 4) * 512:(j % 4) * 512 + OW],
                            ap(bufB, c * T + tc * 128, [[BIG, 128], [1, 128]]),
                            ap(bufA, (og % 2) * (DCH * OW) + c * OW, [[BIG, 128], [1, OW]]),
                            start=(c == 0), stop=False,
                            wait=([e_wb[og % 2], ps_war[j % 4], e_hw[tc], e_bstk]
                                  if c == 0 else None))
            e_mm = P.do("tensor", "matmul",
                        ps[:, (j % 4) * 512:(j % 4) * 512 + OW],
                        hw_all[:, tc * 128:(tc + 1) * 128],
                        bstk_sb[:, og * OW:(og + 1) * OW],
                        start=False, stop=True, inc=(s_pe, 1))
            lastuse_wb[og % 2] = e_mm
            if og + 2 < OG and tc == TC - 1:
                emit_wb_dma(og + 2)
            e_o1 = P.do("vector", "tensor_tensor",
                        osl[:, sl * OW:(sl + 1) * OW],
                        ps[:, (j % 4) * 512:(j % 4) * 512 + OW],
                        bbog[:, :], ALU.add,
                        wait=[e_mm, os_war[sl], e_bbog])
            ps_war[j % 4] = e_o1
            e_o2 = P.do("vector", "tensor_tensor",
                        osl[:, sl * OW:(sl + 1) * OW],
                        osl[:, sl * OW:(sl + 1) * OW],
                        xsl[:, sl * OW:(sl + 1) * OW], ALU.add,
                        wait=[xs_dma[sl]])
            xs_user[sl] = e_o2
            os_war[sl] = P.do("scalar", "dma_start",
                              ap(out_d, tc * 128 * DM + og * OW, [[DM, 128], [1, OW]]),
                              osl[:, sl * OW:(sl + 1) * OW],
                              wait=[e_o2], inc=(s_os[sl], 16), chain=False)
    P.wait("sync", *[(s, P.count[s]) for s in s_os])
    P.emit("p4b")

    return nc


def _prep_inputs(inputs, NQ, T):
    DM = NQ * 128
    DKV = (NQ // 4) * 128
    g = {k: np.ascontiguousarray(np.asarray(v, dtype=np.float32)) for k, v in inputs.items()}
    x = g["L0_attn_hidden_states"]
    shared = {
        "wqT": np.ascontiguousarray(g["L0_attn_Wq"].T),
        "wkT": np.ascontiguousarray(g["L0_attn_Wk"].T),
        "wvT": np.ascontiguousarray(g["L0_attn_Wv"].T),
        "bv": g["L0_attn_bv"].reshape(1, DKV),
        "woT": np.ascontiguousarray(g["L0_attn_Wo"].T),
        "wbT": np.ascontiguousarray(g["L0_ffn_W_base"].T),
        "bb": g["L0_ffn_b_base"].reshape(1, DM),
        "wrT": np.ascontiguousarray(g["L0_ffn_W_router"].T),
        "at": np.ascontiguousarray(g["L0_ffn_A"].reshape(128, DM).T),
        "bstk": np.ascontiguousarray(g["L0_ffn_B"].transpose(0, 2, 1).reshape(128, DM)),
    }
    in_maps = []
    for c in range(8):
        m = dict(shared)
        m["x"] = np.ascontiguousarray(x[c // 4, (c % 4) * T:(c % 4 + 1) * T, :])
        in_maps.append(m)
    return in_maps


def kernel(**inputs):
    NQ, T = 32, 1024
    nc = build(NQ, T)
    in_maps = _prep_inputs(inputs, NQ, T)
    res = run_bass_kernel_spmd(nc, in_maps, core_ids=list(range(8)))
    DM = NQ * 128
    out = np.empty((2, 4 * T, DM), dtype=np.float32)
    for c in range(8):
        out[c // 4, (c % 4) * T:(c % 4 + 1) * T, :] = res.results[c]["out"]
    return out
